# version 02
# baseline (speedup 1.0000x reference)
import contextlib
import numpy as np
import ml_dtypes
import concourse.bass as bass
import concourse.mybir as mybir
from concourse.bass_utils import run_bass_kernel_spmd

F32 = mybir.dt.float32
BF16 = mybir.dt.bfloat16
AF = mybir.ActivationFunctionType
ALU = mybir.AluOpType
AX = mybir.AxisListType

T = 2048
D = 2048
FOUT = 8512
I4 = 4096
NBC = 4352
NH = 64
PP = 64
NS = 128
CH = 128
NCH = T // CH
EXP = 8
HE = 1024
KT = D // 128
NTT = T // 128
EPS_LN = 1e-5
EPS_RMS = 1e-5

ENGS = ("pe", "dve", "act", "pool", "sp")
NLANES = 8
QSEMS = tuple(ENGS) + tuple(
    f"{e}q{i}" for e in ("pool", "sp", "act") for i in range(NLANES))


class Res:
    __slots__ = ("name", "writer", "readers")

    def __init__(self, name):
        self.name = name
        self.writer = None
        self.readers = []


class Op:
    __slots__ = ("fn", "eng", "qsem", "inc_amt", "waits", "need_inc",
                 "sem_val_after", "dbg")

    def __init__(self, fn, eng, qsem, inc_amt, dbg):
        self.fn = fn
        self.eng = eng
        self.qsem = qsem
        self.inc_amt = inc_amt
        self.waits = []
        self.need_inc = False
        self.sem_val_after = 0
        self.dbg = dbg


class EM:

    def __init__(self):
        self.streams = {e: [] for e in ENGS}
        self.nres = 0
        self.dma_count = {e: 0 for e in ("pool", "sp", "act")}
        self.dma_lane_prev = {e: [None] * NLANES for e in ("pool", "sp", "act")}

    def res(self, name=None):
        self.nres += 1
        return Res(name or f"r{self.nres}")

    def op(self, eng, fn, reads=(), writes=(), dma=False, dbg=""):
        lane_prev = None
        if dma:
            j = self.dma_count[eng]
            self.dma_count[eng] = j + 1
            lane = j % NLANES
            qsem = f"{eng}q{lane}"
            lane_prev = self.dma_lane_prev[eng][lane]
        else:
            qsem = eng
        o = Op(fn, eng, qsem, 16 if dma else 1, dbg)
        if dma:
            self.dma_lane_prev[eng][lane] = o
        tgt = []
        if lane_prev is not None:
            tgt.append(lane_prev)
        for r in reads:
            if r.writer is not None:
                tgt.append(r.writer)
        for r in writes:
            if r.writer is not None:
                tgt.append(r.writer)
            tgt.extend(r.readers)
        seen = set()
        for t in tgt:
            if t is o or id(t) in seen:
                continue
            seen.add(id(t))
            if o.qsem == "pe" and t.qsem == "pe":
                continue
            o.waits.append(t)
        for r in reads:
            r.readers.append(o)
        for r in writes:
            r.writer = o
            r.readers = []
        self.streams[eng].append(o)
        return o

    def barrier(self):
        lasts = []
        for s in self.streams.values():
            for t in reversed(s):
                if t.fn is not None and not t.qsem[-1].isdigit():
                    lasts.append(t)
                    break
        for e in ("pool", "sp", "act"):
            for prev in self.dma_lane_prev[e]:
                if prev is not None:
                    lasts.append(prev)
        for eng in ENGS:
            o = Op(None, eng, eng, 1, "barrier")
            o.waits.extend(lasts)
            self.streams[eng].append(o)

    def finalize(self, nc, sems):
        for s in self.streams.values():
            for o in s:
                if o.qsem.endswith("q"):
                    o.need_inc = True
                for t in o.waits:
                    t.need_inc = True
        cum = {q: 0 for q in QSEMS}
        for eng in ENGS:
            for o in self.streams[eng]:
                if o.need_inc:
                    cum[o.qsem] += o.inc_amt
                o.sem_val_after = cum[o.qsem]
        knows = {e: {q: 0 for q in QSEMS} for e in ENGS}
        stats = {"waits": 0, "incs": 0, "ops": 0}

        def emit_stream(engname):
            def run(eh):
                kn = knows[engname]
                for o in self.streams[engname]:
                    stats["ops"] += 1
                    need = {}
                    for t in o.waits:
                        v = t.sem_val_after
                        if v > need.get(t.qsem, 0):
                            need[t.qsem] = v
                    for q in QSEMS:
                        if q in need and kn[q] < need[q]:
                            eh.wait_ge(sems[q], need[q])
                            kn[q] = need[q]
                            stats["waits"] += 1
                    if o.fn is not None:
                        inst = o.fn(eh)
                        if o.need_inc:
                            assert inst is not None, (o.eng, o.dbg)
                            inst.then_inc(sems[o.qsem], o.inc_amt)
                            stats["incs"] += 1
            return run

        with nc.Block() as block:
            block.tensor(emit_stream("pe"))
            block.vector(emit_stream("dve"))
            block.scalar(emit_stream("act"))
            block.gpsimd(emit_stream("pool"))
            block.sync(emit_stream("sp"))
        return stats


class KB:

    def __init__(self, dbg=()):
        self.dbg = set(dbg)
        self.nc = bass.Bass("TRN2", target_bir_lowering=False)
        self.em = EM()
        self.stack = contextlib.ExitStack()
        self.sems = {}
        for q in QSEMS:
            self.sems[q] = self.stack.enter_context(self.nc.semaphore("sem_" + q))
        self.dr = {}

    def din(self, name, shape, dt=F32):
        ap = self.nc.dram_tensor(name, list(shape), dt, kind="ExternalInput").ap()
        self.dr[name] = ap
        return ap

    def dout(self, name, shape, dt=F32):
        ap = self.nc.dram_tensor(name, list(shape), dt, kind="ExternalOutput").ap()
        self.dr[name] = ap
        return ap

    def dint(self, name, shape, dt):
        ap = self.nc.dram_tensor(name, list(shape), dt, kind="Internal").ap()
        self.dr[name] = ap
        return ap

    def sb(self, name, shape, dt, scope=None, side=None):
        st = scope if scope is not None else self.stack
        if side is None:
            return st.enter_context(self.nc.sbuf_tensor(name, list(shape), dt))
        return st.enter_context(
            self.nc.sbuf_tensor(name, list(shape), dt, side=side))

    def close_scope(self, scope):
        self.em.barrier()
        scope.close()

    def ps(self, name, shape, dt=F32):
        return self.stack.enter_context(self.nc.psum_tensor(name, list(shape), dt))

    def build(self):
        em = self.em
        self.x_h = self.din("x", [T, D])
        self.winT_h = self.din("winT", [D, FOUT])
        self.woutT_h = self.din("woutT", [I4, D])
        self.wupT_h = self.din("wupT", [D, EXP * HE])
        self.wdT_h = self.din("wdT", [EXP, HE, D])
        self.wg1T_h = self.din("wg1T", [D, EXP])
        self.wg2T_h = self.din("wg2T", [EXP, EXP])
        self.convw_h = self.din("convw", [128, (NBC // 128) * 4])
        self.convb_h = self.din("convb", [128, NBC // 128])
        self.dtb_h = self.din("dtb", [NH, 1])
        self.alog_h = self.din("alog", [NH, 1])
        self.drep_h = self.din("drep", [1, I4])
        self.nw_h = self.din("nw", [128, I4 // 128])
        self.idf_h = self.din("idf", [128, 128])
        self.tril_h = self.din("tril", [128, 128])
        self.ones_h = self.din("onesrow", [1, 128])
        self.out_h = self.dout("out", [T, D], F32)
        self.zs_h = self.dint("zs", [T, I4], BF16)
        self.xts_h = self.dint("xts", [NCH, I4, CH], BF16)
        self.ffn_h = self.dint("ffn", [T, D], BF16)
        self.ygt_h = self.dint("ygt", [NCH, I4, CH], BF16)
        self.acr_h = self.dint("acr", [NCH, NH * CH], F32)

        self.r_out = em.res("out")
        self.r_zs = [em.res(f"zs{t}") for t in range(NTT)]
        self.r_xts = [em.res(f"xts{c}") for c in range(NCH)]
        self.r_ffn = [em.res(f"ffn{t}") for t in range(NTT)]
        self.r_ygt = [em.res(f"ygt{c}") for c in range(NCH)]
        self.r_acr = [em.res(f"acr{c}") for c in range(NCH)]

        self.idf = self.sb("idf_s", [128, 128], F32)
        self.id16 = self.sb("id16", [128, 128], BF16)
        self.tril = self.sb("tril_s", [128, 128], F32)
        self.ones1 = self.sb("ones1", [1, 128], F32)
        self.ones16 = self.sb("ones16", [1, 128], BF16)
        self.r_idf = em.res("idf")
        self.r_id16 = em.res("id16")
        self.r_tril = em.res("tril")
        self.r_ones1 = em.res("ones1")
        self.r_ones16 = em.res("ones16")
        em.op("sp", lambda e: e.dma_start(self.idf[:, :], self.idf_h),
              writes=[self.r_idf], dma=True)
        em.op("pool", lambda e: e.dma_start(self.id16[:, :], self.idf_h),
              writes=[self.r_id16], dma=True)
        em.op("sp", lambda e: e.dma_start(self.tril[:, :], self.tril_h),
              writes=[self.r_tril], dma=True)
        em.op("sp", lambda e: e.dma_start(self.ones1[:, :], self.ones_h),
              writes=[self.r_ones1], dma=True)
        em.op("pool", lambda e: e.dma_start(self.ones16[:, :], self.ones_h),
              writes=[self.r_ones16], dma=True)

        self.pbank = [self.ps(f"pb{i}", [128, 512], F32) for i in range(8)]
        self.r_pb = [em.res(f"pb{i}") for i in range(8)]

        self.cconst = self.sb("cconst", [128, 8], F32)
        self.r_cconst = em.res("cconst")
        self._cmap = {}
        return self.build_phases()

    def constcol(self, val):
        if val not in self._cmap:
            i = len(self._cmap)
            assert i < 8
            self._cmap[val] = i
            self.em.op("dve", lambda e, i=i, val=val: e.memset(
                self.cconst[:, i:i + 1], float(val)),
                writes=[self.r_cconst], dbg=f"const{val}")
        i = self._cmap[val]
        return self.cconst[:, i:i + 1]

    def build_phases(self):
        import contextlib as _ctx
        self.xns = _ctx.ExitStack()
        self.rscope = _ctx.ExitStack()
        self.phase0()
        if "skipF" not in self.dbg:
            self.phaseF()
        self.phaseA()
        self.close_scope(self.xns)
        if "stopA" in self.dbg:
            self.stub_out()
            self.em.barrier()
            return self
        self.phaseB()
        self.close_scope(self.rscope)
        if "stopB" in self.dbg:
            self.stub_out()
            self.em.barrier()
            return self
        self.phaseD()
        self.em.barrier()
        return self

    def phaseA(self):
        em = self.em
        import contextlib as _ctx
        pas = _ctx.ExitStack()
        pb, r_pb = self.pbank, self.r_pb

        self.bct = self.sb("bct", [128, 2 * T], BF16, self.rscope, side="right")
        self.r_bct = [em.res("bT"), em.res("cT")]
        self.aT = self.sb("aT", [NH, T], F32, self.rscope, side="right")
        self.r_aT = em.res("aT")
        self.dt_tok = self.sb("dt_tok", [128, NCH * NH], BF16, self.rscope,
                              side="right")
        self.r_dtok = em.res("dt_tok")

        wz = [self.sb(f"wz{b}", [128, KT * 512], BF16, pas) for b in range(2)]
        r_wz = [em.res(f"wz{b}") for b in range(2)]
        wx = [self.sb(f"wx{b}", [128, KT * 128], BF16, pas) for b in range(2)]
        r_wx = [em.res(f"wx{b}") for b in range(2)]
        z16 = [self.sb(f"z16_{b}", [128, 512], BF16, pas) for b in range(2)]
        r_z16 = [em.res(f"z16{b}") for b in range(2)]
        xbc_in = [self.sb(f"xbcin{b}", [128, T], BF16, pas) for b in range(2)]
        r_xbc = [em.res(f"xbc{b}") for b in range(2)]
        cacc = [self.sb(f"cacc{b}", [128, T], BF16, pas) for b in range(2)]
        r_cacc = [em.res(f"cacc{b}") for b in range(2)]
        xt16 = [self.sb(f"xt16_{b}", [128, T], BF16, pas) for b in range(2)]
        r_xt16 = [em.res(f"xt16{b}") for b in range(2)]
        convw = self.sb("convw_s", [128, 34 * 4], F32, pas)
        convb = self.sb("convb_s", [128, 34], F32, pas)
        r_cw, r_cb = em.res(), em.res()
        dtr = self.sb("dtr", [NH, T], F32, pas)
        dtt = self.sb("dtt", [NH, T], F32, pas)
        dtv = self.sb("dtv", [NH, T], F32, pas)
        r_dtr, r_dtt, r_dtv = em.res(), em.res(), em.res()
        acol = self.sb("acol", [NH, 2], F32, pas)
        r_acol = em.res()

        em.op("sp", lambda e: e.dma_start(convw[:, :], self.convw_h),
              writes=[r_cw], dma=True)
        em.op("sp", lambda e: e.dma_start(convb[:, :], self.convb_h),
              writes=[r_cb], dma=True)

        for fs in range(8):
            wb, r_wb = wz[fs % 2], r_wz[fs % 2]
            em.op("pool", lambda e, fs=fs, wb=wb: e.dma_start(
                wb[:, :].rearrange("p (k c) -> p k c", k=KT),
                self.winT_h.rearrange("(k p) f -> p k f", k=KT)[
                    :, :, fs * 512:(fs + 1) * 512]),
                writes=[r_wb], dma=True, dbg=f"wz{fs}")
            for tt in range(NTT):
                bi = tt % 2
                def zmm(e, wb=wb, tt=tt, bi=bi):
                    last = None
                    for k in range(KT):
                        last = e.matmul(
                            pb[bi][:, :],
                            self.xnTb[:, :].rearrange("p (k t) -> p k t", k=KT)[
                                :, k, tt * 128:(tt + 1) * 128],
                            wb[:, :].rearrange("p (k c) -> p k c", k=KT)[:, k],
                            start=(k == 0), stop=(k == KT - 1))
                    return last
                em.op("pe", zmm, reads=[r_wb] + self.r_xnT, writes=[r_pb[bi]],
                      dbg=f"zmm{fs}_{tt}")
                zb, r_zb = z16[tt % 2], r_z16[tt % 2]
                em.op("act", lambda e, bi=bi, zb=zb: e.activation(
                    zb[:, :], pb[bi][:, :], AF.Silu),
                    reads=[r_pb[bi]], writes=[r_zb], dbg=f"zsilu{fs}_{tt}")
                em.op("sp", lambda e, fs=fs, tt=tt, zb=zb: e.dma_start(
                    self.zs_h[tt * 128:(tt + 1) * 128,
                              fs * 512:(fs + 1) * 512], zb[:, :]),
                    reads=[r_zb], writes=[self.r_zs[tt]], dma=True,
                    dbg=f"zst{fs}_{tt}")

        for j in range(34):
            wb, r_wb = wx[j % 2], r_wx[j % 2]
            f0 = I4 + j * 128
            em.op("pool", lambda e, f0=f0, wb=wb: e.dma_start(
                wb[:, :].rearrange("p (k c) -> p k c", k=KT),
                self.winT_h.rearrange("(k p) f -> p k f", k=KT)[
                    :, :, f0:f0 + 128]),
                writes=[r_wb], dma=True, dbg=f"wx{j}")
            xb, r_xb = xbc_in[j % 2], r_xbc[j % 2]
            for n in range(4):
                bi = 2 + n % 2
                def xmm(e, wb=wb, n=n, bi=bi):
                    last = None
                    for k in range(KT):
                        last = e.matmul(
                            pb[bi][:, :],
                            wb[:, :].rearrange("p (k c) -> p k c", k=KT)[:, k],
                            self.xnTb[:, :].rearrange("p (k t) -> p k t", k=KT)[
                                :, k, n * 512:(n + 1) * 512],
                            start=(k == 0), stop=(k == KT - 1))
                    return last
                em.op("pe", xmm, reads=[r_wb] + self.r_xnT, writes=[r_pb[bi]],
                      dbg=f"xmm{j}_{n}")
                em.op("dve", lambda e, n=n, bi=bi, xb=xb: e.tensor_copy(
                    xb[:, n * 512:(n + 1) * 512], pb[bi][:, :]),
                    reads=[r_pb[bi]], writes=[r_xb], dbg=f"xev{j}_{n}")
            cb_, r_cc = cacc[j % 2], r_cacc[j % 2]
            em.op("dve", lambda e, j=j, xb=xb, cb_=cb_: e.tensor_scalar(
                cb_[:, :], xb[:, :], convw[:, j * 4 + 3:j * 4 + 4], None,
                ALU.mult),
                reads=[r_xb, r_cw], writes=[r_cc], dbg=f"cv3_{j}")
            for kk in (2, 1, 0):
                s = 3 - kk
                em.op("dve", lambda e, j=j, kk=kk, s=s, xb=xb, cb_=cb_:
                      e.scalar_tensor_tensor(
                          cb_[:, s:T], xb[:, 0:T - s],
                          convw[:, j * 4 + kk:j * 4 + kk + 1],
                          cb_[:, s:T], ALU.mult, ALU.add),
                      reads=[r_xb, r_cw, r_cc], writes=[r_cc], dbg=f"cv{kk}_{j}")
            if j < 32:
                ob, r_ob = xt16[j % 2], r_xt16[j % 2]
                em.op("act", lambda e, j=j, cb_=cb_, ob=ob: e.activation(
                    ob[:, :], cb_[:, :], AF.Silu,
                    bias=convb[:, j:j + 1]),
                    reads=[r_cc, r_cb], writes=[r_ob], dbg=f"cvs{j}")
                em.op("sp", lambda e, j=j, ob=ob: e.dma_start(
                    self.xts_h[:, j * 128:(j + 1) * 128, :].transpose([1, 0, 2]),
                    ob[:, :].rearrange("p (c t) -> p c t", c=NCH)),
                    reads=[r_ob], writes=self.r_xts, dma=True, dbg=f"xst{j}")
            else:
                bi2 = j - 32
                em.op("act", lambda e, j=j, bi2=bi2, cb_=cb_: e.activation(
                    self.bct[:, bi2 * T:(bi2 + 1) * T], cb_[:, :], AF.Silu,
                    bias=convb[:, j:j + 1]),
                    reads=[r_cc, r_cb], writes=[self.r_bct[bi2]], dbg=f"cvs{j}")

        def dtmm(e, n=None):
            last = None
            for k in range(KT):
                last = e.matmul(
                    pb[6][0:NH, 0:512],
                    self.winT_h_dt[:, :].rearrange("p (k c) -> p k c", k=KT)[:, k],
                    self.xnTb[:, :].rearrange("p (k t) -> p k t", k=KT)[
                        :, k, n * 512:(n + 1) * 512],
                    start=(k == 0), stop=(k == KT - 1))
            return last
        wdt = self.sb("wdt", [128, KT * NH], BF16, pas)
        r_wdt = em.res()
        self.winT_h_dt = wdt
        em.op("pool", lambda e: e.dma_start(
            wdt[:, :].rearrange("p (k c) -> p k c", k=KT),
            self.winT_h.rearrange("(k p) f -> p k f", k=KT)[
                :, :, FOUT - NH:FOUT]),
            writes=[r_wdt], dma=True, dbg="wdt")
        for n in range(4):
            em.op("pe", lambda e, n=n: dtmm(e, n=n),
                  reads=[r_wdt] + self.r_xnT, writes=[r_pb[6]], dbg=f"dtmm{n}")
            em.op("dve", lambda e, n=n: e.tensor_copy(
                dtr[:, n * 512:(n + 1) * 512], pb[6][0:NH, 0:512]),
                reads=[r_pb[6]], writes=[r_dtr], dbg=f"dtev{n}")

        dtb = self.sb("dtb_s", [NH, 1], F32, pas)
        r_dtb = em.res()
        em.op("sp", lambda e: e.dma_start(dtb[:, :], self.dtb_h),
              writes=[r_dtb], dma=True)
        em.op("sp", lambda e: e.dma_start(acol[:, 0:1], self.alog_h),
              writes=[r_acol], dma=True)
        onec = self.constcol(1.0)
        rcc = self.r_cconst
        em.op("dve", lambda e: e.tensor_scalar(
            dtr[:, :], dtr[:, :], dtb[:, :], None, ALU.add),
            reads=[r_dtr, r_dtb], writes=[r_dtr])
        em.op("act", lambda e: e.activation(dtt[:, :], dtr[:, :], AF.Abs),
              reads=[r_dtr], writes=[r_dtt], dbg="abs")
        em.op("act", lambda e: e.activation(dtt[:, :], dtt[:, :], AF.Exp,
                                            scale=-1.0),
              reads=[r_dtt], writes=[r_dtt])
        em.op("act", lambda e: e.activation(dtt[:, :], dtt[:, :], AF.Ln,
                                            bias=onec[0:NH, :]),
              reads=[r_dtt, rcc], writes=[r_dtt])
        em.op("dve", lambda e: e.tensor_scalar(
            dtv[:, :], dtr[:, :], 0.0, None, ALU.max),
            reads=[r_dtr], writes=[r_dtv])
        em.op("dve", lambda e: e.tensor_tensor(
            dtv[:, :], dtv[:, :], dtt[:, :], ALU.add),
            reads=[r_dtv, r_dtt], writes=[r_dtv])
        em.op("dve", lambda e: e.tensor_scalar(
            dtv[:, :], dtv[:, :], 100.0, None, ALU.min),
            reads=[r_dtv], writes=[r_dtv], dbg="clip")
        em.op("act", lambda e: e.activation(acol[:, 1:2], acol[:, 0:1], AF.Exp),
              reads=[r_acol], writes=[r_acol])
        em.op("dve", lambda e: e.tensor_scalar(
            acol[:, 1:2], acol[:, 1:2], -1.0, None, ALU.mult),
            reads=[r_acol], writes=[r_acol])
        em.op("dve", lambda e: e.tensor_scalar(
            self.aT[:, :], dtv[:, :], acol[:, 1:2], None, ALU.mult),
            reads=[r_dtv, r_acol], writes=[self.r_aT], dbg="aT")
        for c in range(NCH):
            em.op("pe", lambda e, c=c: e.transpose(
                pb[7][0:128, 0:NH], dtv[:, c * CH:(c + 1) * CH],
                self.idf[0:NH, 0:NH]),
                reads=[r_dtv, self.r_idf], writes=[r_pb[7]], dbg=f"trdt{c}")
            em.op("dve", lambda e, c=c: e.tensor_copy(
                self.dt_tok[:, c * NH:(c + 1) * NH], pb[7][0:128, 0:NH]),
                reads=[r_pb[7]], writes=[self.r_dtok], dbg=f"dtokev{c}")

        if "zxbcdt" in self.dbg:
            oz = self.dout("dbg_z", [T, I4], BF16)
            for tt in range(NTT):
                em.op("sp", lambda e, tt=tt, oz=oz: e.dma_start(
                    oz[tt * 128:(tt + 1) * 128, :],
                    self.zs_h[tt * 128:(tt + 1) * 128, :]),
                    reads=[self.r_zs[tt]], dma=True)
            ox = self.dout("dbg_xts", [NCH, I4, CH], BF16)
            em.op("sp", lambda e, ox=ox: e.dma_start(ox, self.xts_h),
                  reads=self.r_xts, dma=True)
            ob = self.dout("dbg_bct", [128, 2 * T], BF16)
            em.op("sp", lambda e, ob=ob: e.dma_start(ob, self.bct[:, :]),
                  reads=self.r_bct, dma=True)
            od = self.dout("dbg_dtv", [NH, T], F32)
            em.op("sp", lambda e, od=od: e.dma_start(od, dtv[:, :]),
                  reads=[r_dtv], dma=True)
        self.close_scope(pas)

    def phaseB(self):
        em = self.em
        import contextlib as _ctx
        pbs = _ctx.ExitStack()
        pb, r_pb = self.pbank, self.r_pb
        epsc = self.constcol(EPS_RMS)
        rcc = self.r_cconst

        xtc = [self.sb(f"xtc{b}", [128, 32 * CH], BF16, pbs) for b in range(2)]
        r_xtc = [em.res(f"xtc{b}") for b in range(2)]
        zc = [self.sb(f"zc{b}", [128, I4], BF16, pbs) for b in range(2)]
        r_zc = [em.res(f"zc{b}") for b in range(2)]
        Xc = self.sb("Xc", [128, I4], BF16, pbs)
        XD = self.sb("XD", [128, I4], BF16, pbs)
        Xd = self.sb("Xd", [128, I4], BF16, pbs)
        r_Xc, r_XD, r_Xd = em.res(), em.res(), em.res()
        P32 = self.sb("P32", [128, I4], F32, pbs)
        P16 = self.sb("P16", [128, I4], BF16, pbs)
        r_P32, r_P16 = em.res(), em.res()
        y32 = self.sb("y32", [128, I4], F32, pbs)
        r_y32 = em.res()
        yg16 = self.sb("yg16", [128, I4], BF16, pbs)
        r_yg = em.res()
        Rq = [self.sb(f"Rq{b}", [128, 16 * CH], F32, pbs) for b in range(2)]
        r_Rq = [em.res(f"Rq{b}") for b in range(2)]
        Mq = [self.sb(f"Mq{b}", [128, 16 * CH], BF16, pbs) for b in range(2)]
        r_Mq = [em.res(f"Mq{b}") for b in range(2)]
        Gm = self.sb("Gm", [128, CH], BF16, pbs)
        r_Gm = em.res()
        stk = self.sb("stk", [128, CH], F32, pbs)
        r_stk = em.res()
        atok = self.sb("atok", [128, NH], F32, pbs)
        dstok = self.sb("dstok", [128, NH], BF16, pbs)
        eatok = self.sb("eatok", [128, NH], F32, pbs)
        gbc = self.sb("gbc", [128, NH], F32, pbs)
        r_atok, r_dstok, r_eatok, r_gbc = (em.res(), em.res(), em.res(),
                                           em.res())
        onesf = self.sb("onesf", [NH, CH], F32, pbs)
        r_onesf = em.res()
        dfull = self.sb("dfull", [128, I4], BF16, pbs)
        r_dfull = em.res()
        ygstage = [self.sb(f"ygs{b}", [128, 512], BF16, pbs) for b in range(2)]
        r_ygst = [em.res(f"ygs{b}") for b in range(2)]
        btok = self.sb("btok", [128, NS], BF16, pbs)
        r_btok = em.res()
        grow = self.sb("grow", [1, NH], F32, pbs)
        r_grow = em.res()
        self.gscr_h = self.dint("gscr", [NCH, NH], F32)
        r_gscr = [em.res(f"g{c}") for c in range(NCH)]

        em.op("dve", lambda e: e.memset(onesf[:, :], 1.0), writes=[r_onesf])
        em.op("dve", lambda e: e.memset(P32[:, :], 0.0), writes=[r_P32])
        em.op("dve", lambda e: e.memset(P16[:, :], 0.0), writes=[r_P16])
        em.op("pool", lambda e: e.dma_start(
            dfull[:, :], bass.AP(self.drep_h.tensor, 0, [[0, 128], [1, I4]])),
            writes=[r_dfull], dma=True, dbg="dfull")

        em.op("sp", lambda e: e.dma_start(
            xtc[0][:, :].rearrange("p (j t) -> p j t", j=32),
            self.xts_h[0].rearrange("(j p) t -> p j t", j=32)),
            reads=self.r_xts, writes=[r_xtc[0]], dma=True, dbg="ldx0")
        em.op("sp", lambda e: e.dma_start(
            zc[0][:, :], self.zs_h[0:CH, :]),
            reads=[self.r_zs[0]], writes=[r_zc[0]], dma=True, dbg="ldz0")

        nch = NCH
        sec = 99
        for d in self.dbg:
            if d.startswith("nch"):
                nch = int(d[3:])
            if d.startswith("sec"):
                sec = float(d[3:])
        for c in range(nch):
            bi = c % 2
            xb, r_xb = xtc[bi], r_xtc[bi]
            zb, r_zb = zc[bi], r_zc[bi]
            if c + 1 < NCH:
                em.op("sp", lambda e, c=c, bi=bi: e.dma_start(
                    xtc[1 - bi][:, :].rearrange("p (j t) -> p j t", j=32),
                    self.xts_h[c + 1].rearrange("(j p) t -> p j t", j=32)),
                    reads=self.r_xts, writes=[r_xtc[1 - bi]], dma=True,
                    dbg=f"ldx{c+1}")
                em.op("sp", lambda e, c=c, bi=bi: e.dma_start(
                    zc[1 - bi][:, :], self.zs_h[(c + 1) * CH:(c + 2) * CH, :]),
                    reads=[self.r_zs[(c + 1) * CH // 128]],
                    writes=[r_zc[1 - bi]], dma=True, dbg=f"ldz{c+1}")

            if sec < 1:
                continue
            em.op("dve", lambda e, c=c: e.tensor_tensor_scan(
                stk[0:NH, :], onesf[:, :], self.aT[:, c * CH:(c + 1) * CH],
                0.0, ALU.mult, ALU.add),
                reads=[r_onesf, self.r_aT], writes=[r_stk], dbg=f"cums{c}")
            em.op("dve", lambda e: e.tensor_scalar(
                stk[64:128, :], stk[0:NH, :], stk[0:NH, CH - 1:CH], None,
                ALU.subtract),
                reads=[r_stk], writes=[r_stk], dbg="dsT")
            em.op("sp", lambda e, c=c: e.dma_start(
                self.acr_h[c].rearrange("(h l) -> h l", h=NH), stk[0:NH, :]),
                reads=[r_stk], writes=[self.r_acr[c]], dma=True, dbg=f"acr{c}")
            if sec < 1.3:
                continue
            em.op("pe", lambda e: e.transpose(
                pb[0][:, 0:128], stk[:, :], self.idf[:, :]),
                reads=[r_stk, self.r_idf], writes=[r_pb[0]], dbg=f"trstk{c}")
            em.op("dve", lambda e: e.tensor_copy(atok[:, :], pb[0][:, 0:NH]),
                  reads=[r_pb[0]], writes=[r_atok], dbg="atok")
            if sec >= 1.45:
                em.op("act", lambda e: e.activation(
                    dstok[:, :], pb[0][:, NH:128], AF.Exp, scale=-1.0),
                    reads=[r_pb[0], r_atok], writes=[r_dstok], dbg="dstok")
            if sec >= 1.55:
                em.op("act", lambda e: e.activation(
                    eatok[:, :], atok[:, :], AF.Exp),
                    reads=[r_atok], writes=[r_eatok], dbg="eatok")
            if sec < 1.7:
                continue
            em.op("sp", lambda e, c=c: e.dma_start(
                self.gscr_h[c:c + 1, :].rearrange("o h -> h o"),
                stk[0:NH, CH - 1:CH]),
                reads=[r_stk], writes=[r_gscr[c]], dma=True, dbg=f"gst{c}")
            em.op("sp", lambda e, c=c: e.dma_start(
                gbc[:, :],
                bass.AP(self.gscr_h.tensor, c * NH, [[0, 128], [1, NH]])),
                reads=[r_gscr[c]], writes=[r_gbc], dma=True, dbg=f"gbc{c}")
            em.op("act", lambda e: e.activation(gbc[:, :], gbc[:, :], AF.Exp),
                  reads=[r_gbc], writes=[r_gbc], dbg=f"gexp{c}")

            if sec < 2:
                continue
            em.op("pe", lambda e, c=c: e.transpose(
                pb[1][:, 0:128].bitcast(BF16)[:, 0:128],
                self.bct[:, c * CH:(c + 1) * CH], self.id16[:, :]),
                reads=[self.r_bct[0], self.r_id16], writes=[r_pb[1]],
                dbg=f"trB{c}")
            em.op("dve", lambda e: e.tensor_copy(
                btok[:, :], pb[1][:, 0:128].bitcast(BF16)[:, 0:128]),
                reads=[r_pb[1]], writes=[r_btok], dbg="btok")

            em.op("pe", lambda e, c=c: e.matmul(
                pb[2][:, 0:128], self.bct[:, c * CH:(c + 1) * CH],
                self.bct[:, T + c * CH:T + (c + 1) * CH],
                start=True, stop=True),
                reads=self.r_bct, writes=[r_pb[2]], dbg=f"G{c}")
            em.op("dve", lambda e: e.tensor_tensor(
                Gm[:, :], pb[2][:, 0:128], self.tril[:, :], ALU.mult),
                reads=[r_pb[2], self.r_tril], writes=[r_Gm], dbg="Gm")

            if sec < 3:
                continue
            for g in range(8):
                bi2 = 3 + g % 2
                pv = pb[bi2][:, :].bitcast(BF16)

                def trx(e, g=g, pv=pv, xb=xb):
                    last = None
                    for j in range(4):
                        last = e.transpose(
                            pv[:, j * 128:(j + 1) * 128],
                            xb[:, (g * 4 + j) * CH:(g * 4 + j + 1) * CH],
                            self.id16[:, :])
                    return last
                em.op("pe", trx, reads=[r_xb, self.r_id16], writes=[r_pb[bi2]],
                      dbg=f"trx{c}_{g}")
                sl = slice(g * 512, (g + 1) * 512)
                hsl = slice(g * 8, (g + 1) * 8)
                em.op("dve", lambda e, g=g, pv=pv, sl=sl, c=c: e.tensor_tensor(
                    Xc[:, sl].rearrange("p (h q) -> p h q", h=8),
                    pv[:, 0:512].rearrange("p (h q) -> p h q", h=8),
                    self.dt_tok[:, c * NH + g * 8:c * NH + (g + 1) * 8]
                    .unsqueeze(2).broadcast_to([128, 8, PP]),
                    ALU.mult),
                    reads=[r_pb[bi2], self.r_dtok], writes=[r_Xc],
                    dbg=f"xc{c}_{g}")
                em.op("dve", lambda e, g=g, pv=pv, sl=sl: e.tensor_tensor(
                    XD[:, sl], pv[:, 0:512], dfull[:, sl], ALU.mult),
                    reads=[r_pb[bi2], r_dfull], writes=[r_XD],
                    dbg=f"xD{c}_{g}")
                em.op("dve", lambda e, g=g, sl=sl: e.tensor_tensor(
                    Xd[:, sl].rearrange("p (h q) -> p h q", h=8),
                    Xc[:, sl].rearrange("p (h q) -> p h q", h=8),
                    dstok[:, g * 8:(g + 1) * 8].unsqueeze(2)
                    .broadcast_to([128, 8, PP]),
                    ALU.mult),
                    reads=[r_Xc, r_dstok], writes=[r_Xd], dbg=f"xd{c}_{g}")

            if sec < 4:
                continue
            for half in range(2):
                for qq in range(2):
                    q = half * 2 + qq
                    rb, r_rb = Rq[qq], r_Rq[qq]
                    mb, r_mb = Mq[qq], r_Mq[qq]
                    em.op("sp", lambda e, c=c, q=q, rb=rb: e.dma_start(
                        rb[:, :],
                        bass.AP(self.acr_h.tensor, c * NH * CH + q * 16 * CH,
                                [[0, 128], [1, 16 * CH]])),
                        reads=[self.r_acr[c]], writes=[r_rb], dma=True,
                        dbg=f"Rq{c}_{q}")
                    em.op("pool", lambda e, q=q, rb=rb: e.tensor_tensor(
                        rb[:, :].rearrange("p (h l) -> p h l", h=16),
                        rb[:, :].rearrange("p (h l) -> p h l", h=16),
                        atok[:, q * 16:(q + 1) * 16].unsqueeze(2)
                        .broadcast_to([128, 16, CH]),
                        ALU.subtract),
                        reads=[r_rb, r_atok], writes=[r_rb], dbg=f"D{c}_{q}")
                    em.op("dve", lambda e, rb=rb: e.tensor_scalar(
                        rb[:, :], rb[:, :], 0.0, None, ALU.min),
                        reads=[r_rb], writes=[r_rb], dbg=f"Dc{c}_{q}")
                    em.op("act", lambda e, rb=rb, mb=mb: e.activation(
                        mb[:, :], rb[:, :], AF.Exp),
                        reads=[r_rb], writes=[r_mb], dbg=f"E{c}_{q}")
                    em.op("dve", lambda e, mb=mb: e.scalar_tensor_tensor(
                        mb[:, :].rearrange("p (h l) -> p h l", h=16),
                        mb[:, :].rearrange("p (h l) -> p h l", h=16),
                        1.0,
                        Gm[:, :].unsqueeze(1).broadcast_to([128, 16, CH]),
                        ALU.min, ALU.mult),
                        reads=[r_mb, r_Gm], writes=[r_mb], dbg=f"M{c}_{q}")

                hsl = slice(half * 2048, (half + 1) * 2048)
                pbank_h = [pb[4 * half + i] for i in range(4)]
                r_pbh = [r_pb[4 * half + i] for i in range(4)]

                def yoff(e, c=c, half=half, pbank_h=pbank_h):
                    last = None
                    for n in range(4):
                        last = e.matmul(
                            pbank_h[n][:, :],
                            self.bct[:, T + c * CH:T + (c + 1) * CH],
                            P16[:, half * 2048 + n * 512:
                                half * 2048 + (n + 1) * 512],
                            start=True, stop=True)
                    return last
                em.op("pe", yoff, reads=[self.r_bct[1], r_P16],
                      writes=r_pbh, dbg=f"yoff{c}_{half}")
                for n in range(4):
                    em.op("dve", lambda e, n=n, half=half, pbank_h=pbank_h:
                        e.tensor_tensor(
                        y32[:, half * 2048 + n * 512:half * 2048 + (n + 1) * 512]
                        .rearrange("p (h q) -> p h q", h=8),
                        pbank_h[n][:, :].rearrange("p (h q) -> p h q", h=8),
                        eatok[:, half * 32 + n * 8:half * 32 + (n + 1) * 8]
                        .unsqueeze(2).broadcast_to([128, 8, PP]),
                        ALU.mult),
                        reads=[r_pbh[n], r_eatok], writes=[r_y32],
                        dbg=f"ev1_{c}_{half}_{n}")

                def yd(e, c=c, half=half, pbank_h=pbank_h):
                    last = None
                    for n in range(4):
                        last = e.matmul(
                            pbank_h[n][:, :], self.id16[:, :],
                            XD[:, half * 2048 + n * 512:
                               half * 2048 + (n + 1) * 512],
                            start=True, stop=False)
                    for hh in range(32):
                        h = half * 32 + hh
                        q, hq = divmod(hh, 16)
                        n, off = divmod(hh * PP, 512)
                        last = e.matmul(
                            pbank_h[n][:, off:off + PP],
                            Mq[q][:, hq * CH:(hq + 1) * CH],
                            Xc[:, h * PP:(h + 1) * PP],
                            start=False, stop=(off + PP == 512))
                    return last
                em.op("pe", yd,
                      reads=[self.r_id16, r_XD, r_Xc] + r_Mq,
                      writes=r_pbh, dbg=f"yd{c}_{half}")
                for n in range(4):
                    sl2 = slice(half * 2048 + n * 512, half * 2048 + (n + 1) * 512)
                    em.op("dve", lambda e, n=n, sl2=sl2, pbank_h=pbank_h:
                        e.scalar_tensor_tensor(
                        y32[:, sl2], pbank_h[n][:, :], 1.0, y32[:, sl2],
                        ALU.mult, ALU.add),
                        reads=[r_pbh[n], r_y32], writes=[r_y32],
                        dbg=f"ev2_{c}_{half}_{n}")

            if sec < 6:
                continue
            for half in range(2):
                pbank_h = [pb[4 * half + i] for i in range(4)]
                r_pbh = [r_pb[4 * half + i] for i in range(4)]

                def stmm(e, half=half, pbank_h=pbank_h):
                    last = None
                    for n in range(4):
                        last = e.matmul(
                            pbank_h[n][:, :], btok[:, :],
                            Xd[:, half * 2048 + n * 512:
                               half * 2048 + (n + 1) * 512],
                            start=True, stop=True)
                    return last
                em.op("pe", stmm, reads=[r_btok, r_Xd], writes=r_pbh,
                      dbg=f"st{c}_{half}")
                em.op("pool", lambda e, half=half: e.tensor_tensor(
                    P32[:, half * 2048:(half + 1) * 2048]
                    .rearrange("p (h q) -> p h q", h=32),
                    P32[:, half * 2048:(half + 1) * 2048]
                    .rearrange("p (h q) -> p h q", h=32),
                    gbc[:, half * 32:(half + 1) * 32].unsqueeze(2)
                    .broadcast_to([128, 32, PP]),
                    ALU.mult),
                    reads=[r_P32, r_gbc], writes=[r_P32], dbg=f"Pg{c}_{half}")
                for n in range(4):
                    sl2 = slice(half * 2048 + n * 512,
                                half * 2048 + (n + 1) * 512)
                    em.op("dve", lambda e, n=n, sl2=sl2, pbank_h=pbank_h:
                        e.tensor_tensor(
                        P32[:, sl2], P32[:, sl2], pbank_h[n][:, :], ALU.add),
                        reads=[r_pbh[n], r_P32], writes=[r_P32],
                        dbg=f"Pa{c}_{half}_{n}")
                em.op("dve", lambda e, half=half: e.tensor_copy(
                    P16[:, half * 2048:(half + 1) * 2048],
                    P32[:, half * 2048:(half + 1) * 2048]),
                    reads=[r_P32], writes=[r_P16], dbg=f"P16{c}_{half}")

            if sec < 7:
                continue
            em.op("act", lambda e: e.activation(
                yg16[:, :], y32[:, :], AF.Square, accum_out=stk[0:128, 0:1]),
                reads=[r_y32], writes=[r_yg, r_stk], dbg=f"ss{c}")
            em.op("dve", lambda e: e.tensor_scalar(
                stk[:, 1:2], stk[:, 0:1], 1.0 / I4, None, ALU.mult),
                reads=[r_stk], writes=[r_stk])
            em.op("act", lambda e: e.activation(
                stk[:, 1:2], stk[:, 1:2], AF.Ln, bias=epsc),
                reads=[r_stk, rcc], writes=[r_stk])
            em.op("act", lambda e: e.activation(
                stk[:, 1:2], stk[:, 1:2], AF.Exp, scale=-0.5),
                reads=[r_stk], writes=[r_stk], dbg=f"rstd{c}")
            em.op("dve", lambda e: e.tensor_scalar(
                yg16[:, :], y32[:, :], stk[:, 1:2], None, ALU.mult),
                reads=[r_y32, r_stk], writes=[r_yg], dbg=f"yg{c}")
            em.op("dve", lambda e, zb=zb: e.tensor_tensor(
                yg16[:, :], yg16[:, :], zb[:, :], ALU.mult),
                reads=[r_yg, r_zb], writes=[r_yg], dbg=f"ygz{c}")
            for g in range(8):
                bi2 = g % 2
                pv = pb[bi2][:, :].bitcast(BF16)

                def tryg(e, g=g, pv=pv):
                    last = None
                    for j in range(4):
                        last = e.transpose(
                            pv[:, j * 128:(j + 1) * 128],
                            yg16[:, (g * 4 + j) * 128:(g * 4 + j + 1) * 128],
                            self.id16[:, :])
                    return last
                em.op("pe", tryg, reads=[r_yg, self.r_id16],
                      writes=[r_pb[bi2]], dbg=f"tryg{c}_{g}")
                sb_, r_sb = ygstage[bi2], r_ygst[bi2]
                em.op("dve", lambda e, pv=pv, sb_=sb_: e.tensor_copy(
                    sb_[:, :], pv[:, 0:512]),
                    reads=[r_pb[bi2]], writes=[r_sb], dbg=f"evyg{c}_{g}")
                em.op("sp", lambda e, c=c, g=g, sb_=sb_: e.dma_start(
                    self.ygt_h[c, g * 512:(g + 1) * 512, :]
                    .rearrange("(j p) t -> p j t", j=4),
                    sb_[:, :].rearrange("p (j t) -> p j t", j=4)),
                    reads=[r_sb], writes=[self.r_ygt[c]], dma=True,
                    dbg=f"styg{c}_{g}")

        if "yg" in self.dbg:
            o = self.dout("dbg_ygt", [NCH, I4, CH], BF16)
            em.op("sp", lambda e, o=o: e.dma_start(o, self.ygt_h),
                  reads=self.r_ygt, dma=True)
        self.close_scope(pbs)

    def phaseD(self):
        em = self.em
        import contextlib as _ctx
        pds = _ctx.ExitStack()
        pb, r_pb = self.pbank, self.r_pb

        wo16 = self.sb("wo16", [128, 32 * D], BF16, pds)
        r_wo = em.res("wo16")
        nwcol = self.sb("nwcol", [128, 32], F32, pds)
        r_nw = em.res("nw")
        ygcol = [self.sb(f"ygc{b}", [128, 32 * 128], BF16, pds) for b in range(2)]
        r_ygc = [em.res(f"ygc{b}") for b in range(2)]
        x_t = [self.sb(f"xd{b}", [128, D], F32, pds) for b in range(2)]
        r_xt = [em.res(f"xd{b}") for b in range(2)]
        f_t = [self.sb(f"fd{b}", [128, D], BF16, pds) for b in range(2)]
        r_ft = [em.res(f"fd{b}") for b in range(2)]
        o_t = [self.sb(f"od{b}", [128, D], F32, pds) for b in range(2)]
        r_ot = [em.res(f"od{b}") for b in range(2)]

        em.op("sp", lambda e: e.dma_start(nwcol[:, :], self.nw_h),
              writes=[r_nw], dma=True)
        em.op("pool", lambda e: e.dma_start(
            wo16[:, :].rearrange("p (j d) -> p j d", j=32),
            self.woutT_h.rearrange("(j p) d -> p j d", j=32)),
            writes=[r_wo], dma=True, dbg="ldwo")
        for j in range(32):
            em.op("dve", lambda e, j=j: e.tensor_scalar(
                wo16[:, j * D:(j + 1) * D], wo16[:, j * D:(j + 1) * D],
                nwcol[:, j:j + 1], None, ALU.mult),
                reads=[r_wo, r_nw], writes=[r_wo], dbg=f"nwfold{j}")

        em.op("sp", lambda e: e.dma_start(
            ygcol[0][:, :].rearrange("p (j t) -> p j t", j=32),
            self.ygt_h[0].rearrange("(j p) t -> p j t", j=32)),
            reads=[self.r_ygt[0]], writes=[r_ygc[0]], dma=True)
        em.op("sp", lambda e: e.dma_start(x_t[0][:, :], self.x_h[0:128, :]),
              writes=[r_xt[0]], dma=True)
        em.op("sp", lambda e: e.dma_start(f_t[0][:, :], self.ffn_h[0:128, :]),
              reads=[self.r_ffn[0]], writes=[r_ft[0]], dma=True)

        for tt in range(NTT):
            bi = tt % 2
            yb, r_yb = ygcol[bi], r_ygc[bi]
            xb, r_xb = x_t[bi], r_xt[bi]
            fb, r_fb = f_t[bi], r_ft[bi]
            ob, r_ob = o_t[bi], r_ot[bi]
            if tt + 1 < NTT:
                em.op("sp", lambda e, tt=tt, bi=bi: e.dma_start(
                    ygcol[1 - bi][:, :].rearrange("p (j t) -> p j t", j=32),
                    self.ygt_h[tt + 1].rearrange("(j p) t -> p j t", j=32)),
                    reads=[self.r_ygt[tt + 1]], writes=[r_ygc[1 - bi]],
                    dma=True)
                em.op("sp", lambda e, tt=tt, bi=bi: e.dma_start(
                    x_t[1 - bi][:, :],
                    self.x_h[(tt + 1) * 128:(tt + 2) * 128, :]),
                    writes=[r_xt[1 - bi]], dma=True)
                em.op("sp", lambda e, tt=tt, bi=bi: e.dma_start(
                    f_t[1 - bi][:, :],
                    self.ffn_h[(tt + 1) * 128:(tt + 2) * 128, :]),
                    reads=[self.r_ffn[tt + 1]], writes=[r_ft[1 - bi]],
                    dma=True)
            for dsl in range(4):
                bk = dsl
                def omm(e, yb=yb, dsl=dsl, bk=bk):
                    last = None
                    for j in range(32):
                        last = e.matmul(
                            pb[bk][:, :],
                            yb[:, j * 128:(j + 1) * 128],
                            wo16[:, :].rearrange("p (j d) -> p j d", j=32)[
                                :, j, dsl * 512:(dsl + 1) * 512],
                            start=(j == 0), stop=(j == 31))
                    return last
                em.op("pe", omm, reads=[r_yb, r_wo], writes=[r_pb[bk]],
                      dbg=f"omm{tt}_{dsl}")
                sl = slice(dsl * 512, (dsl + 1) * 512)
                em.op("dve", lambda e, sl=sl, bk=bk, ob=ob, xb=xb:
                      e.scalar_tensor_tensor(
                          ob[:, sl], pb[bk][:, :], 1.0, xb[:, sl],
                          ALU.mult, ALU.add),
                      reads=[r_pb[bk], r_xb], writes=[r_ob],
                      dbg=f"oev{tt}_{dsl}")
            em.op("dve", lambda e, ob=ob, fb=fb: e.tensor_tensor(
                ob[:, :], ob[:, :], fb[:, :], ALU.add),
                reads=[r_ob, r_fb], writes=[r_ob], dbg=f"offn{tt}")
            em.op("sp", lambda e, tt=tt, ob=ob: e.dma_start(
                self.out_h[tt * 128:(tt + 1) * 128, :], ob[:, :]),
                reads=[r_ob], writes=[self.r_out], dma=True, dbg=f"out{tt}")
        self.close_scope(pds)

    def phaseF(self):
        em = self.em
        import contextlib as _ctx
        pfs = _ctx.ExitStack()
        pb, r_pb = self.pbank, self.r_pb
        HT = T // 2
        NTH = NTT // 2

        acc16 = self.sb("acc16", [128, NTH * D], BF16, pfs)
        r_acc = [em.res(f"acc{t}") for t in range(NTH)]
        h16 = self.sb("h16", [128, 8 * HT], BF16, pfs)
        r_h16 = [em.res(f"h16_{j}") for j in range(8)]
        wd16 = self.sb("wd16", [128, 8 * D], BF16, pfs)
        r_wd = em.res("wd16")
        wupb = [self.sb(f"wupb{b}", [128, KT * 128], BF16, pfs) for b in range(2)]
        r_wupb = [em.res(f"wupb{b}") for b in range(2)]
        srow = self.sb("srow", [128, HT], BF16, pfs)
        r_srow = em.res("srow")
        bupc = self.sb("bupc_s", [128, EXP * 8], F32, pfs)
        r_bup = em.res("bupc")
        bdn16 = self.sb("bdn16", [EXP, D], BF16, pfs)
        r_bdn = em.res("bdn16")

        self.bupc_h = self.din("bupc", [128, EXP * 8])
        self.bdn_h = self.din("bdn", [EXP, D])
        em.op("sp", lambda e: e.dma_start(bupc[:, :], self.bupc_h),
              writes=[r_bup], dma=True)
        em.op("pool", lambda e: e.dma_start(bdn16[:, :], self.bdn_h),
              writes=[r_bdn], dma=True)

        for th in range(2):
            t_base = th * HT
            for e_i in range(EXP):
                em.op("sp", lambda e, e_i=e_i, t_base=t_base: e.dma_start(
                    srow[:, :],
                    bass.AP(self.scoh_h.tensor, e_i * T + t_base,
                            [[0, 128], [1, HT]])),
                    reads=[self.r_scoh], writes=[r_srow], dma=True,
                    dbg=f"srow{th}_{e_i}")
                em.op("pool", lambda e, e_i=e_i: e.dma_start(
                    wd16[:, :].rearrange("p (j d) -> p j d", j=8),
                    self.wdT_h[e_i].rearrange("(j p) d -> p j d", j=8)),
                    writes=[r_wd], dma=True, dbg=f"wd{th}_{e_i}")

                for ht in range(8):
                    wb, r_wb = wupb[ht % 2], r_wupb[ht % 2]
                    c0 = e_i * HE + ht * 128
                    em.op("pool", lambda e, c0=c0, wb=wb: e.dma_start(
                        wb[:, :].rearrange("p (k c) -> p k c", k=KT),
                        self.wupT_h.rearrange("(k p) c -> p k c", k=KT)[
                            :, :, c0:c0 + 128]),
                        writes=[r_wb], dma=True, dbg=f"wup{th}_{e_i}_{ht}")
                    for n in range(2):
                        bi = n % 2
                        def up_fn(e, wb=wb, n=n, bi=bi, t_base=t_base):
                            last = None
                            for k in range(KT):
                                last = e.matmul(
                                    pb[bi][:, :],
                                    wb[:, :].rearrange(
                                        "p (k c) -> p k c", k=KT)[:, k],
                                    self.xnTb[:, :].rearrange(
                                        "p (k t) -> p k t", k=KT)[
                                        :, k,
                                        t_base + n * 512:t_base + (n + 1) * 512],
                                    start=(k == 0), stop=(k == KT - 1))
                            return last
                        em.op("pe", up_fn, reads=[r_wb] + self.r_xnT,
                              writes=[r_pb[bi]], dbg=f"up{th}_{e_i}_{ht}_{n}")
                        em.op("act", lambda e, ht=ht, n=n, e_i=e_i, bi=bi:
                              e.activation(
                                  h16[:, ht * HT + n * 512:
                                      ht * HT + (n + 1) * 512],
                                  pb[bi][:, :], AF.Silu,
                                  bias=bupc[:, e_i * 8 + ht:e_i * 8 + ht + 1]),
                              reads=[r_pb[bi], r_bup], writes=[r_h16[ht]],
                              dbg=f"silu{th}_{e_i}_{ht}_{n}")
                    em.op("dve", lambda e, ht=ht: e.tensor_tensor(
                        h16[:, ht * HT:(ht + 1) * HT],
                        h16[:, ht * HT:(ht + 1) * HT],
                        srow[:, :], ALU.mult),
                        reads=[r_h16[ht], r_srow], writes=[r_h16[ht]],
                        dbg=f"hsc{th}_{e_i}_{ht}")

                for lt in range(NTH):
                    tt = th * NTH + lt
                    for dsl in range(4):
                        bi = 2 + (lt * 4 + dsl) % 2
                        def dn_fn(e, lt=lt, tt=tt, dsl=dsl, e_i=e_i, bi=bi):
                            last = None
                            if e_i == 0:
                                last = e.matmul(
                                    pb[bi][:, :],
                                    self.scoT[0:8, tt * 128:(tt + 1) * 128],
                                    bdn16[0:8, dsl * 512:(dsl + 1) * 512],
                                    start=True, stop=False)
                            for j in range(8):
                                last = e.matmul(
                                    pb[bi][:, :],
                                    h16[:, j * HT + lt * 128:
                                        j * HT + (lt + 1) * 128],
                                    wd16[:, :].rearrange(
                                        "p (j d) -> p j d", j=8)[
                                        :, j, dsl * 512:(dsl + 1) * 512],
                                    start=(e_i != 0 and j == 0), stop=(j == 7))
                            return last
                        em.op("pe", dn_fn,
                              reads=[r_wd, self.r_scoT, r_bdn] + r_h16,
                              writes=[r_pb[bi]], dbg=f"dn{th}_{e_i}_{lt}_{dsl}")
                        dst = acc16[:, lt * D + dsl * 512:
                                    lt * D + (dsl + 1) * 512]
                        if e_i == 0:
                            em.op("dve", lambda e, dst=dst, bi=bi: e.tensor_copy(
                                dst, pb[bi][:, :]),
                                reads=[r_pb[bi]], writes=[r_acc[lt]],
                                dbg=f"ev0_{th}_{lt}_{dsl}")
                        else:
                            em.op("dve", lambda e, dst=dst, bi=bi:
                                  e.scalar_tensor_tensor(
                                      dst, pb[bi][:, :], 1.0, dst,
                                      ALU.mult, ALU.add),
                                  reads=[r_pb[bi], r_acc[lt]],
                                  writes=[r_acc[lt]],
                                  dbg=f"ev{e_i}_{th}_{lt}_{dsl}")

            for lt in range(NTH):
                tt = th * NTH + lt
                em.op("sp", lambda e, lt=lt, tt=tt: e.dma_start(
                    self.ffn_h[tt * 128:(tt + 1) * 128, :],
                    acc16[:, lt * D:(lt + 1) * D]),
                    reads=[r_acc[lt]], writes=[self.r_ffn[tt]], dma=True,
                    dbg=f"stffn{tt}")

        if "ffn" in self.dbg:
            o = self.dout("dbg_ffn", [T, D], BF16)
            for tt in range(NTT):
                em.op("sp", lambda e, tt=tt, o=o: e.dma_start(
                    o[tt * 128:(tt + 1) * 128, :],
                    self.ffn_h[tt * 128:(tt + 1) * 128, :]),
                    reads=[self.r_ffn[tt]], dma=True)
        self.close_scope(pfs)

    def phase0(self):
        em = self.em
        import contextlib as _ctx
        p0s = _ctx.ExitStack()
        self.xnTb = self.sb("xnTb", [128, KT * T], BF16, self.xns)
        self.r_xnT = [em.res(f"xnT{k}") for k in range(KT)]
        self.scoT = self.sb("scoT", [EXP, T], BF16, self.xns)
        self.r_scoT = em.res("scoT")

        def xnT(k):
            return self.xnTb[:, k * T:(k + 1) * T]

        self.xnT = xnT

        x_tile = [self.sb(f"xt{b}", [128, D], F32, p0s) for b in range(2)]
        r_xtile = [em.res(f"xt{b}") for b in range(2)]
        sq_scr = self.sb("sqscr", [128, D], BF16, p0s)
        r_sq = em.res("sq")
        statsb = [self.sb(f"stats{b}", [128, 8], F32, p0s) for b in range(2)]
        r_statsb = [em.res(f"stats{b}") for b in range(2)]
        xn_stageb = [self.sb(f"xn_stage{b}", [128, D], BF16, p0s)
                     for b in range(2)]
        lo_stageb = [self.sb(f"lo_stage{b}", [128, D], BF16, p0s)
                     for b in range(2)]
        loTb = [self.sb(f"loT{b}", [128, KT * 128], BF16, p0s)
                for b in range(2)]
        r_xnstb = [em.res(f"xnst{b}") for b in range(2)]
        r_lostb = [em.res(f"lost{b}") for b in range(2)]
        r_loTb = [em.res(f"loT{b}") for b in range(2)]

        wg1f = self.sb("wg1f", [128, KT * EXP], F32, p0s)
        wg1hi = self.sb("wg1hi", [128, KT * EXP], BF16, p0s)
        wg1lo = self.sb("wg1lo", [128, KT * EXP], BF16, p0s)
        r_wg1, r_wg1h, r_wg1l = em.res(), em.res(), em.res()
        wg2f = self.sb("wg2f", [EXP, EXP], F32, p0s)
        r_wg2 = em.res()
        em.op("sp", lambda e: e.dma_start(
            wg1f[:, :].rearrange("p (k e) -> p k e", k=KT),
            self.wg1T_h.rearrange("(k p) e -> p k e", k=KT)),
            writes=[r_wg1], dma=True)
        em.op("sp", lambda e: e.dma_start(wg2f[:, :], self.wg2T_h),
              writes=[r_wg2], dma=True)
        em.op("dve", lambda e: e.tensor_copy(wg1hi[:, :], wg1f[:, :]),
              reads=[r_wg1], writes=[r_wg1h])
        em.op("dve", lambda e: e.tensor_tensor(
            wg1lo[:, :], wg1f[:, :], wg1hi[:, :], ALU.subtract),
            reads=[r_wg1, r_wg1h], writes=[r_wg1l])

        l1T = self.sb("l1T", [EXP, T], F32, p0s)
        r_l1 = em.res("l1T")
        l2T = self.sb("l2T", [EXP, T], F32, p0s)
        r_l2 = em.res("l2T")
        lgt = self.sb("lgt", [128, 8], F32, p0s)
        r_lgt = em.res("lgt")
        tkb = self.sb("tkb", [128, 40], F32, p0s)
        r_tk = em.res("tk")

        pb, r_pb = self.pbank, self.r_pb
        epsc = self.constcol(EPS_LN)
        rcc = self.r_cconst

        for tt in range(NTT):
            xb, r_xb = x_tile[tt % 2], r_xtile[tt % 2]
            stats, r_stats = statsb[tt % 2], r_statsb[tt % 2]
            xn_stage, r_xnst = xn_stageb[tt % 2], r_xnstb[tt % 2]
            lo_stage, r_lost = lo_stageb[tt % 2], r_lostb[tt % 2]
            loT, r_loT = loTb[tt % 2], r_loTb[tt % 2]
            em.op("sp", lambda e, tt=tt, xb=xb: e.dma_start(
                xb[:, :], self.x_h[tt * 128:(tt + 1) * 128, :]),
                writes=[r_xb], dma=True, dbg=f"ldx{tt}")
            em.op("dve", lambda e, xb=xb, stats=stats: e.tensor_reduce(
                stats[:, 0:1], xb[:, :], AX.X, ALU.add),
                reads=[r_xb], writes=[r_stats], dbg="lnsum")
            em.op("act", lambda e, xb=xb, stats=stats: e.activation(
                sq_scr[:, :], xb[:, :], AF.Square, accum_out=stats[:, 2:3]),
                reads=[r_xb], writes=[r_sq, r_stats], dbg="lnsq")
            em.op("dve", lambda e, stats=stats: e.tensor_scalar(
                stats[:, 1:2], stats[:, 0:1], 1.0 / D, None, ALU.mult),
                reads=[r_stats], writes=[r_stats])
            em.op("act", lambda e, stats=stats: e.activation(
                stats[:, 3:4], stats[:, 1:2], AF.Square),
                reads=[r_stats], writes=[r_stats])
            em.op("dve", lambda e, stats=stats: e.tensor_scalar(
                stats[:, 3:4], stats[:, 2:3], 1.0 / D, stats[:, 3:4], ALU.mult,
                ALU.subtract), reads=[r_stats], writes=[r_stats], dbg="var")
            em.op("act", lambda e, stats=stats: e.activation(
                stats[:, 4:5], stats[:, 3:4], AF.Ln, bias=epsc),
                reads=[r_stats, rcc], writes=[r_stats])
            em.op("act", lambda e, stats=stats: e.activation(
                stats[:, 4:5], stats[:, 4:5], AF.Exp, scale=-0.5),
                reads=[r_stats], writes=[r_stats], dbg="rstd")
            em.op("dve", lambda e, xb=xb, stats=stats, xn_stage=xn_stage:
                  e.tensor_scalar(
                  xn_stage[:, :], xb[:, :], stats[:, 1:2], stats[:, 4:5],
                  ALU.subtract, ALU.mult),
                  reads=[r_xb, r_stats], writes=[r_xnst], dbg="xn16")
            em.op("dve", lambda e, xb=xb, stats=stats: e.tensor_scalar(
                xb[:, :], xb[:, :], stats[:, 1:2], stats[:, 4:5],
                ALU.subtract, ALU.mult),
                reads=[r_xb, r_stats], writes=[r_xb], dbg="xnf32")
            em.op("dve", lambda e, xb=xb, xn_stage=xn_stage, lo_stage=lo_stage:
                  e.tensor_tensor(
                  lo_stage[:, :], xb[:, :], xn_stage[:, :], ALU.subtract),
                  reads=[r_xb, r_xnst], writes=[r_lost], dbg="lo")

            for g in range(4):
                bi = g % 2
                pv = pb[bi][:, :].bitcast(BF16)

                def tr_fn(e, g=g, pv=pv, xn_stage=xn_stage):
                    last = None
                    for j in range(4):
                        k = g * 4 + j
                        last = e.transpose(pv[:, j * 128:(j + 1) * 128],
                                           xn_stage[:, k * 128:(k + 1) * 128],
                                           self.id16[:, :])
                    return last
                em.op("pe", tr_fn, reads=[r_xnst, self.r_id16], writes=[r_pb[bi]],
                      dbg=f"trxn{tt}_{g}")

                def ev_fn(e, g=g, tt=tt, pv=pv):
                    dst = self.xnTb[:, :].rearrange(
                        "p (k t) -> p k t", k=KT)[:, g * 4:(g + 1) * 4,
                                                  tt * 128:(tt + 1) * 128]
                    return e.tensor_copy(
                        dst, pv[:, 0:512].rearrange("p (j c) -> p j c", j=4))
                em.op("dve", ev_fn, reads=[r_pb[bi]],
                      writes=[self.r_xnT[g * 4 + j] for j in range(4)],
                      dbg=f"evxn{tt}_{g}")

                bi2 = 2 + g % 2
                pv2 = pb[bi2][:, :].bitcast(BF16)

                def trlo_fn(e, g=g, pv2=pv2, lo_stage=lo_stage):
                    last = None
                    for j in range(4):
                        k = g * 4 + j
                        last = e.transpose(pv2[:, j * 128:(j + 1) * 128],
                                           lo_stage[:, k * 128:(k + 1) * 128],
                                           self.id16[:, :])
                    return last
                em.op("pe", trlo_fn, reads=[r_lost, self.r_id16],
                      writes=[r_pb[bi2]], dbg=f"trlo{tt}_{g}")

                def evlo_fn(e, g=g, pv2=pv2, loT=loT):
                    dst = loT[:, :].rearrange("p (k t) -> p k t", k=KT)[
                        :, g * 4:(g + 1) * 4, :]
                    return e.tensor_copy(
                        dst, pv2[:, 0:512].rearrange("p (j c) -> p j c", j=4))
                em.op("dve", evlo_fn, reads=[r_pb[bi2]], writes=[r_loT],
                      dbg=f"evlo{tt}_{g}")

            def gmm_fn(e, tt=tt, loT=loT):
                w_hi = wg1hi[:, :].rearrange("p (k e) -> p k e", k=KT)
                w_lo = wg1lo[:, :].rearrange("p (k e) -> p k e", k=KT)
                lov = loT[:, :].rearrange("p (k t) -> p k t", k=KT)
                last = None
                for k in range(KT):
                    xcol = self.xnTb[:, :].rearrange("p (k t) -> p k t", k=KT)[
                        :, k, tt * 128:(tt + 1) * 128]
                    last = e.matmul(pb[4][0:8, 0:128], w_hi[:, k], xcol,
                                    start=(k == 0), stop=False)
                    last = e.matmul(pb[4][0:8, 0:128], w_lo[:, k], xcol,
                                    start=False, stop=False)
                    last = e.matmul(pb[4][0:8, 0:128], w_hi[:, k], lov[:, k],
                                    start=False, stop=(k == KT - 1))
                return last
            em.op("pe", gmm_fn,
                  reads=[r_wg1h, r_wg1l, r_loT] + self.r_xnT, writes=[r_pb[4]],
                  dbg=f"gmm{tt}")
            em.op("act", lambda e, tt=tt: e.activation(
                l1T[:, tt * 128:(tt + 1) * 128], pb[4][0:8, 0:128], AF.Tanh),
                reads=[r_pb[4]], writes=[r_l1], dbg=f"tanh{tt}")

        for j in range(4):
            em.op("pe", lambda e, j=j: e.matmul(
                pb[5][0:8, 0:512], wg2f[:, :], l1T[:, j * 512:(j + 1) * 512],
                start=True, stop=True),
                reads=[r_wg2, r_l1], writes=[r_pb[5]], dbg=f"g2_{j}")
            em.op("dve", lambda e, j=j: e.tensor_copy(
                l2T[:, j * 512:(j + 1) * 512], pb[5][0:8, 0:512]),
                reads=[r_pb[5]], writes=[r_l2], dbg=f"g2ev{j}")

        for tt in range(NTT):
            em.op("pe", lambda e, tt=tt: e.transpose(
                self.pbank[6][0:128, 0:8], l2T[0:8, tt * 128:(tt + 1) * 128],
                self.idf[0:8, 0:8]),
                reads=[r_l2, self.r_idf], writes=[r_pb[6]], dbg=f"trl2{tt}")
            em.op("dve", lambda e: e.tensor_copy(lgt[:, 0:8],
                                                 self.pbank[6][0:128, 0:8]),
                  reads=[r_pb[6]], writes=[r_lgt])
            em.op("dve", lambda e: e.reduce_max(tkb[:, 0:1], lgt[:, 0:8], AX.X),
                  reads=[r_lgt], writes=[r_tk])
            em.op("dve", lambda e: e.tensor_scalar(
                tkb[:, 8:16], lgt[:, 0:8], tkb[:, 0:1], None, ALU.is_ge),
                reads=[r_lgt, r_tk], writes=[r_tk], dbg="m1")
            em.op("dve", lambda e: e.scalar_tensor_tensor(
                tkb[:, 16:24], tkb[:, 8:16], -1e30, lgt[:, 0:8], ALU.mult,
                ALU.add), reads=[r_tk, r_lgt], writes=[r_tk], dbg="l2m")
            em.op("dve", lambda e: e.reduce_max(tkb[:, 1:2], tkb[:, 16:24], AX.X),
                  reads=[r_tk], writes=[r_tk])
            em.op("dve", lambda e: e.tensor_scalar(
                tkb[:, 24:32], tkb[:, 16:24], tkb[:, 1:2], None, ALU.is_ge),
                reads=[r_tk], writes=[r_tk], dbg="m2")
            em.op("dve", lambda e: e.tensor_tensor(
                tkb[:, 2:3], tkb[:, 0:1], tkb[:, 1:2], ALU.subtract),
                reads=[r_tk], writes=[r_tk])
            em.op("act", lambda e: e.activation(tkb[:, 3:4], tkb[:, 2:3],
                                                AF.Sigmoid),
                  reads=[r_tk], writes=[r_tk], dbg="sig")
            em.op("dve", lambda e: e.tensor_scalar(
                tkb[:, 4:5], tkb[:, 3:4], 1.0, -1.0, ALU.subtract, ALU.mult),
                reads=[r_tk], writes=[r_tk], dbg="p2")
            em.op("dve", lambda e: e.tensor_scalar(
                tkb[:, 8:16], tkb[:, 8:16], tkb[:, 3:4], None, ALU.mult),
                reads=[r_tk], writes=[r_tk])
            em.op("dve", lambda e: e.scalar_tensor_tensor(
                tkb[:, 32:40], tkb[:, 24:32], tkb[:, 4:5], tkb[:, 8:16],
                ALU.mult, ALU.add),
                reads=[r_tk], writes=[r_tk], dbg="scores")
            em.op("pe", lambda e: e.transpose(
                self.pbank[7][0:8, 0:128], tkb[:, 32:40], self.idf[:, :]),
                reads=[r_tk, self.r_idf], writes=[r_pb[7]], dbg=f"trsc{tt}")
            em.op("dve", lambda e, tt=tt: e.tensor_copy(
                self.scoT[0:8, tt * 128:(tt + 1) * 128],
                self.pbank[7][0:8, 0:128]),
                reads=[r_pb[7]], writes=[self.r_scoT])

        self.scoh_h = self.dint("scoh", [EXP, T], BF16)
        self.r_scoh = em.res("scoh")
        em.op("sp", lambda e: e.dma_start(self.scoh_h, self.scoT[:, :]),
              reads=[self.r_scoT], writes=[self.r_scoh], dma=True)

        if "xnT" in self.dbg:
            o = self.dout("dbg_xnT", [KT * 128, T], BF16)
            for k in range(KT):
                em.op("sp", lambda e, k=k, o=o: e.dma_start(
                    o[k * 128:(k + 1) * 128, :], self.xnT(k)),
                    reads=[self.r_xnT[k]], dma=True)
        if "scoT" in self.dbg:
            o = self.dout("dbg_scoT", [EXP, T], BF16)
            em.op("sp", lambda e, o=o: e.dma_start(o, self.scoT[:, :]),
                  reads=[self.r_scoT], dma=True)
        if "l2T" in self.dbg:
            o = self.dout("dbg_l2T", [EXP, T], F32)
            em.op("sp", lambda e, o=o: e.dma_start(o, l2T[:, :]),
                  reads=[r_l2], dma=True)
        self.close_scope(p0s)

    def stub_out(self):
        em = self.em
        import contextlib as _ctx
        sos = _ctx.ExitStack()
        x_tile = [self.sb(f"so{b}", [128, D], F32, sos) for b in range(2)]
        r_xtile = [em.res(f"so{b}") for b in range(2)]
        for tt in range(NTT):
            xb, r_xb = x_tile[tt % 2], r_xtile[tt % 2]
            em.op("sp", lambda e, tt=tt, xb=xb: e.dma_start(
                xb[:, :], self.x_h[tt * 128:(tt + 1) * 128, :]),
                writes=[r_xb], dma=True)
            em.op("sp", lambda e, tt=tt, xb=xb: e.dma_start(
                self.out_h[tt * 128:(tt + 1) * 128, :], xb[:, :]),
                reads=[r_xb], writes=[self.r_out], dma=True)
        self.close_scope(sos)

    def finish(self):
        st = self.em.finalize(self.nc, self.sems)
        self.stack.close()
        return st


def host_inputs(core_x, w):
    f32 = np.float32
    d = {
        "x": np.ascontiguousarray(core_x, dtype=f32),
        "winT": w["winT"], "woutT": w["woutT"], "wupT": w["wupT"],
        "wdT": w["wdT"], "wg1T": w["wg1T"], "wg2T": w["wg2T"],
        "convw": w["convw"], "convb": w["convb"], "dtb": w["dtb"],
        "alog": w["alog"], "drep": w["drep"], "nw": w["nw"],
        "idf": w["idf"], "tril": w["tril"], "onesrow": w["onesrow"],
        "bupc": w["bupc"], "bdn": w["bdn"],
    }
    return d


def shared_weights(inputs):
    f32 = np.float32
    asc = np.ascontiguousarray
    w = {}
    w["winT"] = asc(np.asarray(inputs["L0_attn_in_proj_w"], f32).T)
    w["woutT"] = asc(np.asarray(inputs["L0_attn_out_proj_w"], f32).T)
    wup = np.asarray(inputs["L0_ffn_Wup"], f32)
    w["wupT"] = asc(wup.transpose(2, 0, 1).reshape(D, EXP * HE))
    wdn = np.asarray(inputs["L0_ffn_Wdown"], f32)
    w["wdT"] = asc(wdn.transpose(0, 2, 1))
    w["wg1T"] = asc(np.asarray(inputs["L0_ffn_Wg1"], f32).T)
    w["wg2T"] = asc(np.asarray(inputs["L0_ffn_Wg2"], f32).T)
    cw = np.asarray(inputs["L0_attn_conv_w"], f32)[:, 0, :]
    w["convw"] = asc(cw.reshape(NBC // 128, 128, 4).transpose(1, 0, 2)
                     .reshape(128, (NBC // 128) * 4))
    cb = np.asarray(inputs["L0_attn_conv_b"], f32)
    w["convb"] = asc(cb.reshape(NBC // 128, 128).T)
    w["dtb"] = asc(np.asarray(inputs["L0_attn_dt_bias"], f32).reshape(NH, 1))
    w["alog"] = asc(np.asarray(inputs["L0_attn_A_log"], f32).reshape(NH, 1))
    w["drep"] = asc(np.repeat(np.asarray(inputs["L0_attn_D"], f32), PP)
                    .reshape(1, I4))
    nw = np.asarray(inputs["L0_attn_norm_w"], f32)
    w["nw"] = asc(nw.reshape(I4 // 128, 128).T)
    bup = np.asarray(inputs["L0_ffn_bup"], f32)
    w["bupc"] = asc(np.concatenate(
        [bup[e].reshape(8, 128).T for e in range(EXP)], axis=1))
    w["bdn"] = asc(np.asarray(inputs["L0_ffn_bdown"], f32))
    w["idf"] = np.eye(128, dtype=f32)
    w["tril"] = np.triu(np.ones((128, 128), f32))
    w["onesrow"] = np.ones((1, 128), f32)
    return w


def kernel(**inputs):
    kb = KB().build()
    st = kb.finish()
    w = shared_weights(inputs)
    x = np.asarray(inputs["L0_attn_hidden_states"], np.float32)
    in_maps = [host_inputs(x[i], w) for i in range(8)]
    res = run_bass_kernel_spmd(kb.nc, in_maps, core_ids=list(range(8)))
    out = np.stack([np.asarray(r["out"]) for r in res.results], axis=0)
    return out
